# version 01
# baseline (speedup 1.0000x reference)
import numpy as np
import ml_dtypes
import concourse.bass as bass
import concourse.mybir as mybir
from concourse.bass_utils import run_bass_kernel_spmd
from contextlib import ExitStack

F32 = mybir.dt.float32
BF16 = mybir.dt.bfloat16
AF = mybir.ActivationFunctionType
ALU = mybir.AluOpType
AX = mybir.AxisListType

RC = 1.5 * 2 ** 23
EPS = 1e-5
NCORES = 8
NB = 2
S = 2048
D = 1024
HID = 4096
NT = S // 128
NDT = D // 128
NCT = HID // 128
WIN = 256
NWIN = S // WIN
QB = 512
NQB = S // QB
NKT = S // 128
NHG = 2
FPG = 4


class _Op:
    __slots__ = ("eng", "fn", "waits", "flag", "dma_sem", "post", "val")

    def __init__(self, eng, fn, waits):
        self.eng = eng
        self.fn = fn
        self.waits = [w for w in waits if w is not None]
        self.flag = False
        self.dma_sem = None
        self.post = 0
        self.val = 0


class _Sem:
    __slots__ = ("name", "handle", "count")

    def __init__(self, name):
        self.name = name
        self.handle = None
        self.count = 0


class Trace:
    ENGS = ("sync", "gpsimd", "tensor", "vector", "scalar")

    def __init__(self):
        self.ops = []
        self.esem = {e: _Sem("es_" + e) for e in ("tensor", "vector", "scalar", "gpsimd")}
        self.dsems = {}

    def dma_sem(self, name):
        if name not in self.dsems:
            self.dsems[name] = _Sem("ds_" + name)
        return self.dsems[name]

    def add(self, eng, fn, waits=()):
        op = _Op(eng, fn, waits)
        self.ops.append(op)
        return op

    def dma(self, eng, fn, sem, waits=()):
        w = [x for x in waits if x is not None]
        if sem.count > 0:
            w.append(("d", sem, sem.count * 16))
        op = self.add(eng, fn, w)
        sem.count += 1
        op.dma_sem = sem
        op.post = sem.count * 16
        return op

    @staticmethod
    def ev(op):
        if op is None:
            return None
        if op.dma_sem is not None:
            return ("d", op.dma_sem, op.post)
        op.flag = True
        return ("e", op)

    def emit(self, nc, ctx):
        cnt = {e: 0 for e in self.esem}
        for op in self.ops:
            if op.dma_sem is None and op.flag:
                cnt[op.eng] += 1
                op.val = cnt[op.eng]
        for s in list(self.esem.values()) + list(self.dsems.values()):
            s.handle = ctx.enter_context(nc.semaphore(s.name))
        block = ctx.enter_context(nc.Block())

        def make_body(ename):
            my = [op for op in self.ops if op.eng == ename]

            def body(e):
                waited = {}
                for op in my:
                    for w in op.waits:
                        if w[0] == "d":
                            sem, val = w[1], w[2]
                        else:
                            if w[1].eng == ename:
                                continue
                            sem, val = self.esem[w[1].eng], w[1].val
                        if waited.get(id(sem), -1) < val:
                            waited[id(sem)] = val
                            e.wait_ge(sem.handle, val)
                    ins = op.fn(e)
                    if op.dma_sem is not None:
                        ins.then_inc(op.dma_sem.handle, 16)
                    else:
                        if op.flag:
                            ins.then_inc(self.esem[op.eng].handle, 1)
                        if ename in ("vector", "scalar", "gpsimd"):
                            e.drain()
            return body

        for ename in self.ENGS:
            if any(op.eng == ename for op in self.ops):
                getattr(block, ename)(make_body(ename))


E = Trace.ev


def build_program(phases=("f1", "at", "f2"), nb=NB):
    nc = bass.Bass("TRN2", target_bir_lowering=False, debug=False)

    x_in = nc.dram_tensor("x", [nb * S, D], F32, kind="ExternalInput").ap()
    w_qkv = nc.dram_tensor("w_qkv", [D, 3 * D], F32, kind="ExternalInput").ap()
    bqcol = nc.dram_tensor("bqcol", [128, 24], F32, kind="ExternalInput").ap()
    bqv_row = nc.dram_tensor("bqv_row", [1, D], F32, kind="ExternalInput").ap()
    bout_row = nc.dram_tensor("bout_row", [1, D], F32, kind="ExternalInput").ap()
    w_out = nc.dram_tensor("w_out", [D, D], F32, kind="ExternalInput").ap()
    w1 = nc.dram_tensor("w1", [HID, D], F32, kind="ExternalInput").ap()
    cw_in = nc.dram_tensor("cw_pt", [128, 3 * NCT], F32, kind="ExternalInput").ap()
    cb_in = nc.dram_tensor("cb_pt", [128, NCT], F32, kind="ExternalInput").ap()
    w2 = nc.dram_tensor("w2", [D, HID], F32, kind="ExternalInput").ap()
    y_out = nc.dram_tensor("y", [nb * S, D], F32, kind="ExternalOutput").ap()

    wq1t = nc.dram_tensor("wq1t_scr", [NCT, 128, D], BF16, kind="Internal").ap()
    wq2t = nc.dram_tensor("wq2t_scr", [NCT, 128, D], BF16, kind="Internal").ap()
    ident = nc.inline_tensor(np.eye(128, dtype=np.float32), name="identf32").ap()

    tr = Trace()

    with ExitStack() as ctx:
        def sb(name, shape, dtype):
            return ctx.enter_context(nc.sbuf_tensor(name, shape, dtype))

        X = sb("X", [128, NT * D], F32)
        CW = sb("CW", [128, NCT * 3], F32)
        CB = sb("CB", [128, NCT], F32)
        BQCOL = sb("BQCOL", [128, 24], F32)
        IDF = sb("IDF", [128, 128], F32)
        ONEF = sb("ONEF", [128, 1], F32)
        ONEFR = sb("ONEFR", [1, 128], F32)
        ONEBR = sb("ONEBR", [1, 128], BF16)
        ONEBC = sb("ONEBC", [128, 1], BF16)
        RCCOL = sb("RCCOL", [128, 1], F32)
        NRCCOL = sb("NRCCOL", [128, 1], F32)
        WSC = sb("WSC", [128, 4], F32)
        BQVB = sb("BQVB", [128, D], F32)
        BOUTB = sb("BOUTB", [128, D], F32)
        STATS = sb("STATS", [128, 64], F32)
        AMAX1 = sb("AMAX1", [128, NT], F32)
        S1C = sb("S1C", [128, NT], F32)
        F1C = sb("F1C", [128, NT], F32)
        AMAX2 = sb("AMAX2", [128, NT], F32)
        S2CC = sb("S2CC", [128, NT], F32)
        FAC2 = sb("FAC2", [128, NT], F32)
        SCR16 = sb("SCR16", [128, NT], F32)
        F1ROW = sb("F1ROW", [1, S], F32)
        S2ROW = sb("S2ROW", [1, WIN], F32)
        ROWR = sb("ROWR", [1, QB], F32)
        ROWB = sb("ROWB", [1, QB], BF16)
        ROWS = sb("ROWS", [1, 64], F32)

        AR_A = sb("AR_A", [128, NDT * S], BF16)
        AR_B = sb("AR_B", [128, NCT * WIN], F32)
        AR_C = sb("AR_C", [128, NCT * WIN], BF16)
        WQ1B = sb("WQ1B", [128, 4 * D], BF16)
        WQ2B = sb("WQ2B", [128, 8 * D], BF16)
        HS = sb("HS", [128, 2 * (WIN + 2)], F32)
        TA = sb("TA", [128, 2 * WIN], F32)
        BCS = TA.bitcast(F32)[0:64, 0:QB]
        F1B = sb("F1B", [128, WIN + 2], F32)
        S2B = sb("S2B", [128, WIN], F32)
        ACC = sb("ACC", [128, WIN], F32)
        QR = sb("QR", [128, D], F32)
        BST = sb("BST", [128, 12], F32)
        AVT = sb("AVT", [128, FPG * QB], BF16)
        NEGB = sb("NEGB", [128, 8], F32)
        AVST = HS.bitcast(BF16)[0:64, 0:QB]
        QSQ = F1B.bitcast(BF16)[:, 0:QB]

        LN1 = AR_C.bitcast(F32)
        XQ = AR_B.bitcast(BF16)
        H2 = AR_B
        XQT = AR_A
        XQ2 = AR_C
        L2T = AR_A
        L2TOK = AR_B.bitcast(BF16)
        ARBB = AR_B.bitcast(BF16)
        KT = AR_C
        VBLK = 8 * 65
        E_OFF = NKT * VBLK
        QT_OFF = E_OFF + 4096

        PS1 = ctx.enter_context(nc.psum_tensor("PS1", [128, 1024], F32))
        PS2 = ctx.enter_context(nc.psum_tensor("PS2", [128, 1024], F32))
        PA = ctx.enter_context(nc.psum_tensor("PA", [128, 512], F32))
        PB = ctx.enter_context(nc.psum_tensor("PB", [128, 512], F32))
        PC = ctx.enter_context(nc.psum_tensor("PC", [128, 512], F32))
        PD = ctx.enter_context(nc.psum_tensor("PD", [128, 512], F32))

        sem_x = tr.dma_sem("xld")
        sem_y = tr.dma_sem("yst")
        sem_m1 = [tr.dma_sem("wq1_%d" % i) for i in range(3)]
        sem_m2 = [tr.dma_sem("wq2_%d" % i) for i in range(3)]
        sem_tp = [tr.dma_sem("tp_%d" % i) for i in range(2)]
        sem_wl = [tr.dma_sem("wl_%d" % i) for i in range(2)]
        sem_sw = [tr.dma_sem("sw_%d" % i) for i in range(2)]
        sem_qk = [tr.dma_sem("qk_%d" % i) for i in range(2)]
        sem_sh = tr.dma_sem("shift")

        tr.add("vector", lambda e: e.memset(ONEF[:], 1.0))
        tr.add("vector", lambda e: e.memset(ONEFR[:], 1.0))
        tr.add("vector", lambda e: e.memset(ONEBR[:], 1.0))
        tr.add("vector", lambda e: e.memset(ONEBC[:], 1.0))
        tr.add("vector", lambda e: e.memset(RCCOL[:], RC))
        op_consts = tr.add("vector", lambda e: e.memset(NRCCOL[:], -RC))

        ld_id = tr.dma("sync", lambda e: e.dma_start(IDF[:], ident), sem_wl[0])
        ld_cw = tr.dma("sync", lambda e: e.dma_start(CW[:], cw_in), sem_wl[0])
        ld_cb = tr.dma("sync", lambda e: e.dma_start(CB[:], cb_in), sem_wl[0])
        ld_bq = tr.dma("sync", lambda e: e.dma_start(BQCOL[:], bqcol), sem_wl[0])
        ld_bv = tr.dma("sync", lambda e: e.dma_start(F1ROW[0:1, 0:D], bqv_row), sem_wl[0])
        ld_bo = tr.dma("sync", lambda e: e.dma_start(F1ROW[0:1, D:2 * D], bout_row), sem_wl[0])

        bc_prev = None
        for (dst, base) in ((BQVB, 0), (BOUTB, D)):
            for c0 in (0, 512):
                m = tr.add("tensor", lambda e, base=base, c0=c0: e.matmul(
                    PC[:, 0:512], ONEFR[0:1, :], F1ROW[0:1, base + c0:base + c0 + 512],
                    start=True, stop=True),
                    [E(ld_bv), E(ld_bo), E(op_consts), E(bc_prev)])
                bc_prev = tr.add("scalar", lambda e, dst=dst, c0=c0: e.copy(
                    dst[:, c0:c0 + 512], PC[:, 0:512]), [E(m)])
        op_bias_b = bc_prev

        W1LD = AR_B
        WQBF = AR_C
        STG = AR_A
        PSUMW = STATS

        def prep_matrix(src_ap, nblk, blk_cols, ws_col, inv_col, inv_scale, scratch,
                        is_w1, rows_c0, prev_events):
            lds = []
            reds = []
            for i in range(nblk):
                w = list(prev_events) if i < 2 else [E(reds[i - 2])]
                ld = tr.dma("sync", lambda e, i=i: e.dma_start(
                    W1LD[:, (i % 2) * blk_cols:(i % 2 + 1) * blk_cols],
                    src_ap.rearrange("(n p) c -> n p c", p=128)[i]), sem_wl[i % 2], w)
                lds.append(ld)
                red = tr.add("vector", lambda e, i=i: e.tensor_reduce(
                    PSUMW[:, i:i + 1],
                    W1LD[:, (i % 2) * blk_cols:(i % 2 + 1) * blk_cols],
                    AX.X, ALU.add, apply_absolute_value=True), [E(ld)])
                reds.append(red)
            redall = tr.add("vector", lambda e: e.tensor_reduce(
                PSUMW[:, 40:41], PSUMW[:, 0:nblk], AX.X, ALU.add), [E(reds[-1])])
            msum = tr.add("tensor", lambda e: e.matmul(
                PC[0:1, 0:1], PSUMW[:, 40:41], ONEF[:, 0:1], start=True, stop=True),
                [E(redall)] + list(prev_events))
            c0 = rows_c0
            t1 = tr.add("scalar", lambda e: e.copy(ROWS[0:1, c0:c0 + 1], PC[0:1, 0:1]), [E(msum)])
            t2 = tr.add("vector", lambda e: e.tensor_scalar(
                ROWS[0:1, c0 + 1:c0 + 2], ROWS[0:1, c0:c0 + 1], 1.0 / float(HID * D), 1e-5,
                ALU.mult, ALU.max), [E(t1)])
            t3 = tr.add("vector", lambda e: e.reciprocal(
                ROWS[0:1, c0 + 2:c0 + 3], ROWS[0:1, c0 + 1:c0 + 2]), [])
            t4 = tr.add("vector", lambda e: e.reciprocal(
                ROWS[0:1, c0 + 3:c0 + 4], ROWS[0:1, c0 + 2:c0 + 3]), [])
            t4b = tr.add("vector", lambda e: e.tensor_scalar(
                ROWS[0:1, c0 + 4:c0 + 5], ROWS[0:1, c0 + 3:c0 + 4], inv_scale, None,
                ALU.mult), [])
            mws = tr.add("tensor", lambda e: e.matmul(
                PC[:, 0:1], ONEFR[0:1, :], ROWS[0:1, c0 + 2:c0 + 3], start=True, stop=True),
                [E(t4b), E(t1)])
            cws = tr.add("scalar", lambda e: e.copy(WSC[:, ws_col:ws_col + 1], PC[:, 0:1]), [E(mws)])
            minv = tr.add("tensor", lambda e: e.matmul(
                PC[:, 1:2], ONEFR[0:1, :], ROWS[0:1, c0 + 4:c0 + 5], start=True, stop=True),
                [E(cws)])
            cinv = tr.add("scalar", lambda e: e.copy(WSC[:, inv_col:inv_col + 1], PC[:, 1:2]),
                          [E(minv)])
            prev_q = {0: None, 1: None}
            stores = []
            tps = []
            for i in range(nblk):
                wslot = (i % 2) * blk_cols
                w = [E(prev_q[i % 2])] if prev_q[i % 2] is not None else [E(redall)]
                ld = tr.dma("sync", lambda e, i=i, wslot=wslot: e.dma_start(
                    W1LD[:, wslot:wslot + blk_cols],
                    src_ap.rearrange("(n p) c -> n p c", p=128)[i]), sem_wl[i % 2], w)
                q1 = tr.add("vector", lambda e, wslot=wslot: e.tensor_scalar(
                    W1LD[:, wslot:wslot + blk_cols], W1LD[:, wslot:wslot + blk_cols],
                    WSC[:, ws_col:ws_col + 1], RC, ALU.mult, ALU.add),
                    [E(ld), E(cinv)] + (list(prev_events) if i == 0 else []))
                q2 = tr.add("vector", lambda e, wslot=wslot: e.tensor_scalar(
                    W1LD[:, wslot:wslot + blk_cols], W1LD[:, wslot:wslot + blk_cols],
                    -RC, -1.0, ALU.add, ALU.max), [])
                q3 = tr.add("vector", lambda e, wslot=wslot: e.tensor_scalar(
                    WQBF[:, wslot:wslot + blk_cols], W1LD[:, wslot:wslot + blk_cols],
                    1.0, None, ALU.min),
                    (list(prev_events) if i == 0 else []) + ([E(tps[i - 2])] if i >= 2 else []))
                prev_q[i % 2] = q3
                ntile = blk_cols // 128
                tp = tr.dma("scalar", lambda e, wslot=wslot, ntile=ntile: e.dma_start_transpose(
                    STG[:, 0:blk_cols].rearrange("p (n f) -> p n f", n=ntile),
                    WQBF[:, wslot:wslot + blk_cols].rearrange("p (n f) -> p n f", n=ntile)),
                    sem_tp[i % 2],
                    [E(q3)] + ([E(stores[-1])] if stores else list(prev_events)))
                tps.append(tp)
                if is_w1:
                    st = tr.dma("sync", lambda e, i=i: e.dma_start(
                        scratch[i], STG[:, 0:blk_cols]), sem_sw[0 if is_w1 else 1], [E(tp)])
                else:
                    st = tr.dma("sync", lambda e, i=i: e.dma_start(
                        scratch[:, :, i * 128:(i + 1) * 128].rearrange("ct p fm -> p ct fm"),
                        STG[:, 0:blk_cols].rearrange("p (ct fm) -> p ct fm", fm=128)),
                        sem_sw[1], [E(tp)])
                stores.append(st)
            return stores, q3, cinv

        w1_stores, w1_lastq, w1_cinv = prep_matrix(
            w1, NCT, D, 2, 0, 1.0, wq1t, True, 8, [E(ld_id), E(op_bias_b)])
        w2_stores, w2_lastq, w2_cinv = prep_matrix(
            w2, NDT, HID, 3, 1, 0.5, wq2t, False, 16,
            [E(w1_lastq), E(w1_stores[-1])])
        w1_done = ("d", tr.dsems["sw_0"], tr.dsems["sw_0"].count * 16)
        w2_done = ("d", tr.dsems["sw_1"], tr.dsems["sw_1"].count * 16)
        prep_done = [w1_done, w2_done, E(w2_lastq)]

        def layernorm_tile(tt, out_ap, waits):
            tr.add("vector", lambda e, tt=tt: e.bn_stats(
                BST[:, 0:6], X[:, tt * D:tt * D + 512]), waits)
            tr.add("vector", lambda e, tt=tt: e.bn_stats(
                BST[:, 6:12], X[:, tt * D + 512:tt * D + 1024]), [])
            tr.add("vector", lambda e, tt=tt: e.bn_aggr(
                STATS[:, 2 * tt:2 * tt + 2], BST[:]), [])
            tr.add("vector", lambda e, tt=tt: e.tensor_scalar(
                STATS[:, 32 + tt:33 + tt], STATS[:, 2 * tt + 1:2 * tt + 2], EPS, None,
                ALU.add), [])
            t2 = tr.add("vector", lambda e, tt=tt: e.reciprocal(
                STATS[:, 48 + tt:49 + tt], STATS[:, 32 + tt:33 + tt]), [])
            t3 = tr.add("scalar", lambda e, tt=tt: e.activation(
                STATS[:, 32 + tt:33 + tt], STATS[:, 48 + tt:49 + tt], AF.Sqrt), [E(t2)])
            t4 = tr.add("vector", lambda e, tt=tt: e.tensor_mul(
                STATS[:, 48 + tt:49 + tt], STATS[:, 2 * tt:2 * tt + 1],
                STATS[:, 32 + tt:33 + tt]), [E(t3)])
            t5 = tr.add("vector", lambda e, tt=tt: e.tensor_scalar(
                STATS[:, 48 + tt:49 + tt], STATS[:, 48 + tt:49 + tt], -1.0, None,
                ALU.mult), [])
            return tr.add("scalar", lambda e, tt=tt, out_ap=out_ap: e.activation(
                out_ap, X[:, tt * D:(tt + 1) * D], AF.Identity,
                bias=STATS[:, 48 + tt:49 + tt], scale=STATS[:, 32 + tt:33 + tt]), [E(t5)])

        ffn_state = {"m1dma": {}, "m2dma": {}}

        def ffn(x_ready, arena_free):
            round_ops = []
            for tt in range(NT):
                lnslot = (tt % 2) * D
                lnw = list(x_ready) + list(arena_free) if tt == 0 else []
                if tt >= 2:
                    lnw.append(E(round_ops[-1]))
                ap_op = layernorm_tile(tt, LN1[:, lnslot:lnslot + D], lnw)
                tr.add("vector", lambda e, tt=tt, lnslot=lnslot: e.tensor_reduce(
                    AMAX1[:, tt:tt + 1], LN1[:, lnslot:lnslot + D], AX.X, ALU.max,
                    apply_absolute_value=True), [E(ap_op)])
                tr.add("vector", lambda e, tt=tt: e.tensor_scalar(
                    SCR16[:, tt:tt + 1], AMAX1[:, tt:tt + 1], 1e-5, None, ALU.max), [])
                tr.add("vector", lambda e, tt=tt: e.reciprocal(
                    S1C[:, tt:tt + 1], SCR16[:, tt:tt + 1]), [])
                c3 = tr.add("vector", lambda e, tt=tt: e.tensor_scalar(
                    S1C[:, tt:tt + 1], S1C[:, tt:tt + 1], 127.0, None, ALU.mult), [])
                r1 = tr.add("scalar", lambda e, tt=tt, lnslot=lnslot: e.activation(
                    QR[:], LN1[:, lnslot:lnslot + D], AF.Identity,
                    bias=RCCOL[:, 0:1], scale=S1C[:, tt:tt + 1]),
                    [E(c3), E(op_consts)])
                r2 = tr.add("vector", lambda e, tt=tt: e.tensor_scalar(
                    XQ[:].rearrange("p (dt n c) -> p dt n c", dt=NDT, n=NT)[:, :, tt, :],
                    QR[:].rearrange("p (dt c) -> p dt c", c=128), -RC, None, ALU.add),
                    [E(r1)] + (list(arena_free) if tt == 0 else []))
                round_ops.append(r2)
            f1a = tr.add("vector", lambda e: e.reciprocal(F1C[:], S1C[:]), [E(round_ops[-1])])
            f1b_op = tr.add("vector", lambda e: e.tensor_scalar(
                F1C[:], F1C[:], WSC[:, 0:1], None, ALU.mult), [w1_done, E(w1_cinv)])
            f1row_ops = []
            prev_cp = None
            for tt in range(NT):
                tp = tr.add("tensor", lambda e, tt=tt: e.transpose(
                    PD[0:1, 0:128], F1C[:, tt:tt + 1], IDF[:]),
                    [E(f1b_op), E(ld_id), E(prev_cp)])
                prev_cp = tr.add("scalar", lambda e, tt=tt: e.copy(
                    F1ROW[0:1, tt * 128:(tt + 1) * 128], PD[0:1, 0:128]), [E(tp)])
                f1row_ops.append(prev_cp)
            tp_ops = []
            for dt in range(NDT):
                tp_ops.append(tr.dma("scalar", lambda e, dt=dt: e.dma_start_transpose(
                    XQT[:, dt * S:(dt + 1) * S].rearrange("p (n f) -> p n f", n=NT),
                    XQ[:, dt * S:(dt + 1) * S].rearrange("p (n f) -> p n f", n=NT)),
                    sem_tp[dt % 2], [E(round_ops[-1])] + list(arena_free)))
            xqt_done = [E(t) for t in tp_ops]

            prev_round = None
            prev_m2_stop = None
            prev_sbc = None
            prev_evacs = []
            evac_by_slot = {}
            psg = [PS2[:, 0:512], PS2[:, 512:1024], PA[:], PB[:]]
            for w in range(NWIN):
                cin0 = max(WIN * w - 1, 0)
                cin1 = min(WIN * w + WIN + 1, S)
                wid = cin1 - cin0
                off = WIN * w - cin0
                jl0 = 1 if w == 0 else 0
                jr1 = WIN - 1 if w == NWIN - 1 else WIN

                fm = tr.add("tensor", lambda e, cin0=cin0, wid=wid: e.matmul(
                    PC[:, 0:wid], ONEFR[0:1, :], F1ROW[0:1, cin0:cin0 + wid],
                    start=True, stop=True),
                    [E(f1row_ops[-1]), E(prev_sbc), E(prev_m2_stop)])
                fcp = tr.add("scalar", lambda e, wid=wid: e.copy(
                    F1B[:, 0:wid], PC[:, 0:wid]), [E(fm)])

                silu_ops = []
                cur_dma = None
                for ct in range(NCT):
                    g = ct // 2
                    if ct % 2 == 0:
                        bufsel = g % 2
                        dw = [ffn_state["m1dma"].get(bufsel)]
                        if w == 0 and ct == 0:
                            dw += [w1_done] + list(x_ready)
                        cur_dma = tr.dma("sync", lambda e, g=g, bufsel=bufsel: e.dma_start(
                            WQ1B[:, bufsel * 2 * D:(bufsel + 1) * 2 * D]
                            .rearrange("p (ct c) -> p ct c", c=D),
                            wq1t[2 * g:2 * g + 2].rearrange("ct p c -> p ct c")),
                            sem_m1[g % 3], dw)
                    lct = ct % 2
                    bufsel = (ct // 2) % 2
                    pslot = (ct % 2) * 512
                    for dt in range(NDT):
                        waits = []
                        if dt == 0:
                            waits = [("d", cur_dma.dma_sem, cur_dma.post),
                                     E(evac_by_slot.get(pslot))]
                            if w == 0 and ct <= 1:
                                waits += xqt_done
                        mm = tr.add("tensor", lambda e, dt=dt, cin0=cin0, wid=wid,
                                    pslot=pslot, bufsel=bufsel, lct=lct: e.matmul(
                            PS1[:, pslot:pslot + wid],
                            WQ1B[:, bufsel * 2 * D + lct * D + dt * 128:
                                 bufsel * 2 * D + lct * D + (dt + 1) * 128],
                            XQT[:, dt * S + cin0:dt * S + cin0 + wid],
                            start=(dt == 0), stop=(dt == NDT - 1)), waits)
                    if ct % 2 == 1:
                        ffn_state["m1dma"][bufsel] = E(mm)
                    hslot = (ct % 2) * (WIN + 2)
                    ev = tr.add("vector", lambda e, wid=wid, pslot=pslot, hslot=hslot:
                                e.tensor_mul(HS[:, hslot:hslot + wid],
                                             PS1[:, pslot:pslot + wid], F1B[:, 0:wid]),
                                [E(mm), E(fcp)]
                                + ([E(prev_round)] if (w > 0 and ct <= 1) else [])
                                + ([E(silu_ops[ct - 2])] if ct >= 2 else []))
                    evac_by_slot[pslot] = ev
                    taslot = (ct % 2) * WIN
                    cwc = ct * 3
                    tR = tr.add("scalar", lambda e, hslot=hslot, taslot=taslot,
                                jr1=jr1, off=off, cwc=cwc: e.activation(
                        TA[:, taslot:taslot + jr1], HS[:, hslot + off + 1:hslot + off + 1 + jr1],
                        AF.Identity, scale=CW[:, cwc + 2:cwc + 3]), [E(ev), E(ld_cw)])
                    after_tr = [tR]
                    if jr1 < WIN:
                        z = tr.add("vector", lambda e, taslot=taslot, jr1=jr1: e.memset(
                            TA[:, taslot + jr1:taslot + WIN], 0.0), [E(tR)])
                        after_tr.append(z)
                    tL = tr.add("vector", lambda e, hslot=hslot, taslot=taslot,
                                jl0=jl0, off=off, cwc=cwc: e.scalar_tensor_tensor(
                        TA[:, taslot + jl0:taslot + WIN],
                        HS[:, hslot + off + jl0 - 1:hslot + off + WIN - 1],
                        CW[:, cwc:cwc + 1],
                        TA[:, taslot + jl0:taslot + WIN], ALU.mult, ALU.add),
                        [E(o) for o in after_tr])
                    tM = tr.add("vector", lambda e, hslot=hslot, taslot=taslot,
                                off=off, cwc=cwc: e.scalar_tensor_tensor(
                        TA[:, taslot:taslot + WIN], HS[:, hslot + off:hslot + off + WIN],
                        CW[:, cwc + 1:cwc + 2],
                        TA[:, taslot:taslot + WIN], ALU.mult, ALU.add), [])
                    sl = tr.add("scalar", lambda e, ct=ct, taslot=taslot: e.activation(
                        H2[:, ct * WIN:(ct + 1) * WIN], TA[:, taslot:taslot + WIN],
                        AF.Silu, bias=CB[:, ct:ct + 1]), [E(tM), E(ld_cb)])
                    silu_ops.append(sl)

                red = tr.add("vector", lambda e: e.tensor_reduce(
                    ACC[:], H2[:].rearrange("p (ct j) -> p j ct", j=WIN), AX.X, ALU.max,
                    apply_absolute_value=True), [E(silu_ops[-1])])
                prev = red
                for i in range(2):
                    tp = tr.add("tensor", lambda e, i=i: e.transpose(
                        PD[:, 0:128], ACC[:, i * 128:(i + 1) * 128], IDF[:]),
                        [E(prev)])
                    rd = tr.add("vector", lambda e, i=i, w=w: e.tensor_reduce(
                        AMAX2[:, 2 * w + i:2 * w + i + 1], PD[:, 0:128], AX.X, ALU.max),
                        [E(tp)])
                    prev = rd
                tr.add("vector", lambda e, w=w: e.tensor_scalar(
                    SCR16[:, 0:2], AMAX2[:, 2 * w:2 * w + 2], 1e-5, None, ALU.max), [E(prev)])
                tr.add("vector", lambda e, w=w: e.reciprocal(
                    S2CC[:, 2 * w:2 * w + 2], SCR16[:, 0:2]), [])
                s2c = tr.add("vector", lambda e, w=w: e.tensor_scalar(
                    S2CC[:, 2 * w:2 * w + 2], S2CC[:, 2 * w:2 * w + 2], 127.0, None,
                    ALU.mult), [])
                tr.add("vector", lambda e, w=w: e.reciprocal(
                    FAC2[:, 2 * w:2 * w + 2], S2CC[:, 2 * w:2 * w + 2]), [])
                fc2 = tr.add("vector", lambda e, w=w: e.tensor_scalar(
                    FAC2[:, 2 * w:2 * w + 2], FAC2[:, 2 * w:2 * w + 2], WSC[:, 1:2], None,
                    ALU.mult), [w2_done, E(w2_cinv)])
                rowcp = None
                for i in range(2):
                    tps2 = tr.add("tensor", lambda e, i=i, w=w: e.transpose(
                        PD[0:1, 0:128], S2CC[:, 2 * w + i:2 * w + i + 1], IDF[:]),
                        [E(s2c), E(rowcp)])
                    rowcp = tr.add("scalar", lambda e, i=i: e.copy(
                        S2ROW[0:1, i * 128:(i + 1) * 128], PD[0:1, 0:128]), [E(tps2)])
                sbm = tr.add("tensor", lambda e: e.matmul(
                    PC[:, 0:WIN], ONEFR[0:1, :], S2ROW[0:1, 0:WIN], start=True, stop=True),
                    [E(rowcp), E(fcp)])
                sbc = tr.add("scalar", lambda e: e.copy(S2B[:], PC[:, 0:WIN]), [E(sbm)])
                prev_sbc = sbc

                round_last = None
                for ct in range(NCT):
                    mu = tr.add("vector", lambda e, ct=ct: e.tensor_mul(
                        H2[:, ct * WIN:(ct + 1) * WIN], H2[:, ct * WIN:(ct + 1) * WIN],
                        S2B[:]), [E(sbc)] if ct == 0 else [])
                    ra = tr.add("scalar", lambda e, ct=ct: e.activation(
                        QR[:, 0:WIN], H2[:, ct * WIN:(ct + 1) * WIN], AF.Identity,
                        bias=RCCOL[:, 0:1]),
                        [E(mu)] + ([E(prev_m2_stop)] if ct == 0 else [])
                        + (list(arena_free) if (w == 0 and ct == 0) else []))
                    round_last = tr.add("scalar", lambda e, ct=ct: e.activation(
                        XQ2[:, ct * WIN:(ct + 1) * WIN], QR[:, 0:WIN], AF.Identity,
                        bias=NRCCOL[:, 0:1]), [])
                prev_round = round_last

                cur_dma2 = None
                for ct in range(NCT):
                    g = ct // 2
                    if ct % 2 == 0:
                        bufsel = g % 2
                        dw = [ffn_state["m2dma"].get(bufsel)]
                        if w == 0 and ct == 0:
                            dw += [w2_done] + list(x_ready)
                        cur_dma2 = tr.dma("sync", lambda e, g=g, bufsel=bufsel: e.dma_start(
                            WQ2B[:, bufsel * 2 * D:(bufsel + 1) * 2 * D]
                            .rearrange("p (ct c) -> p ct c", c=D),
                            wq2t[2 * g:2 * g + 2].rearrange("ct p c -> p ct c")),
                            sem_m2[g % 3], dw)
                    lct = ct % 2
                    bufsel = (ct // 2) % 2
                    for tt in range(2):
                        for fb in range(2):
                            waits = []
                            if tt == 0 and fb == 0:
                                waits = [("d", cur_dma2.dma_sem, cur_dma2.post)]
                                if ct == 0:
                                    waits += [E(round_last)] + [E(o) for o in prev_evacs]
                            mm2 = tr.add("tensor", lambda e, ct=ct, tt=tt, fb=fb,
                                         bufsel=bufsel, lct=lct: e.matmul(
                                psg[tt * 2 + fb],
                                XQ2[:, ct * WIN + tt * 128:ct * WIN + (tt + 1) * 128],
                                WQ2B[:, bufsel * 2 * D + lct * D + fb * 512:
                                     bufsel * 2 * D + lct * D + (fb + 1) * 512],
                                start=(ct == 0), stop=(ct == NCT - 1)), waits)
                    if ct % 2 == 1:
                        ffn_state["m2dma"][bufsel] = E(mm2)
                prev_m2_stop = mm2
                prev_evacs = []
                for tt in range(2):
                    for fb in range(2):
                        gtt = 2 * w + tt
                        evr = tr.add("vector", lambda e, tt=tt, fb=fb, gtt=gtt:
                                     e.scalar_tensor_tensor(
                            X[:, gtt * D + fb * 512:gtt * D + (fb + 1) * 512],
                            psg[tt * 2 + fb], FAC2[:, gtt:gtt + 1],
                            X[:, gtt * D + fb * 512:gtt * D + (fb + 1) * 512],
                            ALU.mult, ALU.add), [E(prev_m2_stop), E(fc2)])
                        prev_evacs.append(evr)
            return prev_evacs

        at_state = {"exp_prev": None, "av_prev": None, "pair_evacs": None, "pe_prev": None}

        def attn(x_ready, arena_free):
            ln_last = None
            for tt in range(NT):
                ln_last = layernorm_tile(
                    tt, L2TOK[:].rearrange("p (dt n c) -> p dt n c", dt=NDT, n=NT)[:, :, tt, :],
                    (list(x_ready) + list(arena_free)) if tt == 0 else [])
            ltp = []
            for dt in range(NDT):
                ltp.append(tr.dma("scalar", lambda e, dt=dt: e.dma_start_transpose(
                    L2T[:, dt * S:(dt + 1) * S].rearrange("p (n f) -> p n f", n=NT),
                    L2TOK[:, dt * S:(dt + 1) * S].rearrange("p (n f) -> p n f", n=NT)),
                    sem_tp[dt % 2], [E(ln_last)] + list(arena_free)))
            l2t_done = [E(t) for t in ltp]

            preadd = []
            for tt in range(NT):
                pa_ = tr.add("vector", lambda e, tt=tt: e.tensor_add(
                    X[:, tt * D:(tt + 1) * D], X[:, tt * D:(tt + 1) * D], BOUTB[:]),
                    [E(ln_last), E(op_bias_b)] if tt == 0 else [])
                preadd.append(pa_)

            last_proj = []
            for hg in range(NHG):
                guard = l2t_done if hg == 0 else [E(o) for o in last_proj[-8:]]
                kt_last_evac = None
                ksq_prev = None
                kw_consumed = None
                k2m_init = tr.add("vector", lambda e: e.memset(ROWS[0:1, 0:16], 0.0), guard)
                for fp in range(FPG):
                    pair = hg * FPG + fp
                    kw = tr.dma("gpsimd", lambda e, pair=pair: e.dma_start(
                        WQ1B[:, 0:D].rearrange("p (dt c) -> p dt c", c=128),
                        w_qkv.rearrange("(dt p) c -> p dt c", p=128)
                        [:, :, D + pair * 128:D + (pair + 1) * 128]), sem_qk[0],
                        guard + [E(kw_consumed)])
                    for Nb in range(4):
                        for dt in range(NDT):
                            waits = []
                            if dt == 0:
                                waits = [("d", kw.dma_sem, kw.post), E(kt_last_evac)]
                                if Nb == 0 and fp == 0:
                                    waits += guard
                            km = tr.add("tensor", lambda e, fp=fp, Nb=Nb, dt=dt: e.matmul(
                                PC[:, 0:512], WQ1B[:, dt * 128:(dt + 1) * 128],
                                L2T[:, dt * S + Nb * 512:dt * S + (Nb + 1) * 512],
                                start=(dt == 0), stop=(dt == NDT - 1)), waits)
                        ke = tr.add("scalar", lambda e, fp=fp, Nb=Nb, pair=pair: e.activation(
                            KT[:, fp * S + Nb * 512:fp * S + (Nb + 1) * 512], PC[:, 0:512],
                            AF.Identity, bias=BQCOL[:, 8 + pair:9 + pair]),
                            [E(km), E(ld_bq)])
                        kt_last_evac = ke
                        sq = tr.add("scalar", lambda e, fp=fp, Nb=Nb: e.activation(
                            QSQ[:], KT[:, fp * S + Nb * 512:fp * S + (Nb + 1) * 512],
                            AF.Square), [E(ke), E(ksq_prev)])
                        prev_rd = None
                        for hh in range(2):
                            qm = tr.add("tensor", lambda e, hh=hh: e.matmul(
                                PD[0:1, 0:512], ONEBC[hh * 64:hh * 64 + 64, 0:1],
                                QSQ[hh * 64:hh * 64 + 64, :], start=True, stop=True),
                                [E(sq), E(at_state["pe_prev"]), E(ksq_prev), E(prev_rd)])
                            rd = tr.add("vector", lambda e: e.tensor_reduce(
                                ROWS[0:1, 32:33], PD[0:1, 0:512], AX.X, ALU.max), [E(qm)])
                            prev_rd = rd
                            mx = tr.add("vector", lambda e, fp=fp, hh=hh: e.tensor_tensor(
                                ROWS[0:1, 2 * fp + hh:2 * fp + hh + 1],
                                ROWS[0:1, 2 * fp + hh:2 * fp + hh + 1],
                                ROWS[0:1, 32:33], ALU.max), [E(rd), E(k2m_init)])
                            ksq_prev = mx
                    kw_consumed = ke
                tr.add("scalar", lambda e: e.activation(
                    ROWS[0:1, 16:24], ROWS[0:1, 0:8], AF.Ln), [E(ksq_prev)])
                mk2 = tr.add("scalar", lambda e: e.activation(
                    ROWS[0:1, 16:24], ROWS[0:1, 16:24], AF.Exp, scale=0.5), [])

                vw = tr.dma("gpsimd", lambda e, hg=hg: e.dma_start(
                    WQ2B[:, 0:4 * D].rearrange("p (dt c) -> p dt c", c=512),
                    w_qkv.rearrange("(dt p) c -> p dt c", p=128)
                    [:, :, 2 * D + hg * 512:2 * D + (hg + 1) * 512]), sem_qk[1],
                    [E(kt_last_evac)] + guard)
                vones = tr.add("vector", lambda e: e.memset(
                    ARBB[:, 0:NKT * VBLK].rearrange("p (t c) -> p t c", c=VBLK)
                    [:, :, 64:VBLK:65], 1.0), guard)
                v_evacs = []
                for tt in range(NKT):
                    for dt in range(NDT):
                        waits = []
                        if dt == 0:
                            waits = [("d", vw.dma_sem, vw.post), E(vones), E(ksq_prev)]
                            if tt >= 2:
                                waits.append(E(v_evacs[tt - 2]))
                        vm = tr.add("tensor", lambda e, tt=tt, dt=dt: e.matmul(
                            PC[:, 0:512] if tt % 2 == 0 else PD[:, 0:512],
                            L2T[:, dt * S + tt * 128:dt * S + (tt + 1) * 128],
                            WQ2B[:, dt * 512:(dt + 1) * 512],
                            start=(dt == 0), stop=(dt == NDT - 1)), waits)
                    ve = tr.add("vector", lambda e, tt=tt, hg=hg: e.tensor_add(
                        ARBB[:, tt * VBLK:(tt + 1) * VBLK]
                        .rearrange("p (h c) -> p h c", c=65)[:, :, 0:64],
                        (PC[:, 0:512] if tt % 2 == 0 else PD[:, 0:512])
                        .rearrange("p (h c) -> p h c", c=64),
                        BQVB[:, hg * 512:(hg + 1) * 512].rearrange("p (h c) -> p h c", c=64)),
                        [E(vm)])
                    v_evacs.append(ve)

                qw_consumed = None
                wo_dma = None
                for qb in range(NQB):
                    qt_last = None
                    negb_last = None
                    for fp in range(FPG):
                        pair = hg * FPG + fp
                        qw = tr.dma("gpsimd", lambda e, pair=pair: e.dma_start(
                            WQ1B[:, D:2 * D].rearrange("p (dt c) -> p dt c", c=128),
                            w_qkv.rearrange("(dt p) c -> p dt c", p=128)
                            [:, :, pair * 128:(pair + 1) * 128]), sem_qk[0],
                            [E(qw_consumed), E(v_evacs[-1])])
                        for dt in range(NDT):
                            waits = []
                            if dt == 0:
                                waits = [("d", qw.dma_sem, qw.post), E(qt_last),
                                         E(v_evacs[-1]), E(at_state["pe_prev"])]
                            qm = tr.add("tensor", lambda e, fp=fp, dt=dt, qb=qb: e.matmul(
                                PC[:, 0:512], WQ1B[:, D + dt * 128:D + (dt + 1) * 128],
                                L2T[:, dt * S + qb * 512:dt * S + (qb + 1) * 512],
                                start=(dt == 0), stop=(dt == NDT - 1)), waits)
                        qw_consumed = qm
                        qe = tr.add("scalar", lambda e, fp=fp, pair=pair: e.activation(
                            ARBB[:, QT_OFF + fp * 512:QT_OFF + (fp + 1) * 512], PC[:, 0:512],
                            AF.Identity, bias=BQCOL[:, pair:pair + 1]), [E(qm)])
                        qt_last = qe
                        sq = tr.add("scalar", lambda e, fp=fp: e.activation(
                            QSQ[:], ARBB[:, QT_OFF + fp * 512:QT_OFF + (fp + 1) * 512],
                            AF.Square), [E(qe)])
                        for hh in range(2):
                            qm2 = tr.add("tensor", lambda e, hh=hh: e.matmul(
                                PD[0:1, 0:512], ONEBC[hh * 64:hh * 64 + 64, 0:1],
                                QSQ[hh * 64:hh * 64 + 64, :], start=True, stop=True),
                                [E(sq), E(at_state["pe_prev"]), E(at_state.get("rd_prev"))])
                            rd = tr.add("vector", lambda e: e.tensor_reduce(
                                ROWS[0:1, 33:34], PD[0:1, 0:512], AX.X, ALU.max), [E(qm2)])
                            at_state["rd_prev"] = rd
                            s1_ = tr.add("scalar", lambda e: e.activation(
                                ROWS[0:1, 34:35], ROWS[0:1, 33:34], AF.Ln), [E(rd)])
                            s2_ = tr.add("scalar", lambda e: e.activation(
                                ROWS[0:1, 34:35], ROWS[0:1, 34:35], AF.Exp, scale=0.5), [])
                            tr.add("vector", lambda e, fp=fp, hh=hh: e.tensor_mul(
                                ROWS[0:1, 40 + 2 * fp + hh:41 + 2 * fp + hh],
                                ROWS[0:1, 34:35],
                                ROWS[0:1, 16 + 2 * fp + hh:17 + 2 * fp + hh]),
                                [E(s2_), E(mk2)])
                            negb_last = tr.add("vector", lambda e, fp=fp, hh=hh: e.tensor_scalar(
                                ROWS[0:1, 40 + 2 * fp + hh:41 + 2 * fp + hh],
                                ROWS[0:1, 40 + 2 * fp + hh:41 + 2 * fp + hh],
                                -0.125, None, ALU.mult), [])
                    nbm = tr.add("tensor", lambda e: e.matmul(
                        PC[:, 0:8], ONEFR[0:1, :], ROWS[0:1, 40:48], start=True, stop=True),
                        [E(negb_last), E(qt_last)])
                    nbc = tr.add("scalar", lambda e: e.copy(NEGB[:], PC[:, 0:8]), [E(nbm)])

                    for fp in range(FPG):
                        av_last = {0: None, 1: None}
                        exp_ops = {}
                        for g in range(8):
                            sc_last = None
                            for kk in range(2):
                                kt = 2 * g + kk
                                for hh in range(2):
                                    waits = []
                                    if kk == 0 and hh == 0:
                                        if g >= 1:
                                            waits = [E(exp_ops[(g - 1, 0)]),
                                                     E(exp_ops[(g - 1, 1)])]
                                        else:
                                            waits = [E(nbc), E(at_state["exp_prev"])]
                                    sc_last = tr.add("tensor", lambda e, fp=fp, kt=kt, hh=hh,
                                                     kk=kk: e.matmul(
                                        (PS1 if hh == 0 else PS2)[:, kk * 512:(kk + 1) * 512],
                                        KT[hh * 64:hh * 64 + 64,
                                           fp * S + kt * 128:fp * S + (kt + 1) * 128],
                                        ARBB[hh * 64:hh * 64 + 64,
                                             QT_OFF + fp * 512:QT_OFF + (fp + 1) * 512],
                                        start=True, stop=True,
                                        tile_position=(hh * 64, 0)), waits)
                            eset = g % 2
                            for hh in range(2):
                                ew = [E(sc_last)]
                                if g >= 2:
                                    ew.append(E(av_last[eset]))
                                elif at_state["av_prev"] is not None:
                                    ew.append(at_state["av_prev"])
                                exp_ops[(g, hh)] = tr.add(
                                    "scalar", lambda e, fp=fp, hh=hh, eset=eset: e.activation(
                                        ARBB[:, E_OFF + eset * 2048 + hh * 1024:
                                             E_OFF + eset * 2048 + (hh + 1) * 1024],
                                        (PS1 if hh == 0 else PS2)[:],
                                        AF.Exp, scale=0.125,
                                        bias=NEGB[:, 2 * fp + hh:2 * fp + hh + 1]), ew)
                            for kk in range(2):
                                kt = 2 * g + kk
                                for hh in range(2):
                                    waits = []
                                    if kk == 0:
                                        waits = [E(exp_ops[(g, 0)]), E(exp_ops[(g, 1)])]
                                        if g == 0 and hh == 0 and at_state["pair_evacs"]:
                                            waits += at_state["pair_evacs"]
                                    av = tr.add("tensor", lambda e, fp=fp, kt=kt, hh=hh,
                                                eset=eset, kk=kk: e.matmul(
                                        (PA if hh == 0 else PB)[0:65, :],
                                        ARBB[:, kt * VBLK + (2 * fp + hh) * 65:
                                             kt * VBLK + (2 * fp + hh) * 65 + 65],
                                        ARBB[:, E_OFF + eset * 2048 + hh * 1024 + kk * 512:
                                             E_OFF + eset * 2048 + hh * 1024 + (kk + 1) * 512],
                                        start=(kt == 0), stop=(kt == NKT - 1)), waits)
                            av_last[eset] = av
                        at_state["exp_prev"] = exp_ops[(7, 1)]
                        at_state["av_prev"] = E(av)

                        pair_evacs = []
                        shift_ev = None
                        prev_bc = at_state.get("bc_prev")
                        for hh in range(2):
                            ps = PA if hh == 0 else PB
                            rc_ = tr.add("vector", lambda e, ps=ps: e.reciprocal(
                                ROWR[0:1, :], ps[64:65, :]), [E(av)])
                            rb_ = tr.add("vector", lambda e: e.tensor_copy(
                                ROWB[0:1, :], ROWR[0:1, :]), [E(prev_bc)])
                            bm = tr.add("tensor", lambda e: e.matmul(
                                PC[0:64, :], ONEBR[0:1, 0:64], ROWB[0:1, :],
                                start=True, stop=True), [E(rb_), E(prev_bc)])
                            bc_ = tr.add("scalar", lambda e: e.copy(BCS[:], PC[0:64, :]),
                                         [E(bm)])
                            prev_bc = bc_
                            if hh == 0:
                                mu = tr.add("vector", lambda e, fp=fp, ps=ps: e.tensor_mul(
                                    AVT[0:64, fp * QB:(fp + 1) * QB], ps[0:64, :], BCS[:]),
                                    [E(bc_)])
                            else:
                                mu = tr.add("vector", lambda e, ps=ps: e.tensor_mul(
                                    AVST[:], ps[0:64, :], BCS[:]),
                                    [E(bc_), at_state.get("shift_prev")])
                                shift_ev = tr.dma("gpsimd", lambda e, fp=fp: e.dma_start(
                                    AVT[64:128, fp * QB:(fp + 1) * QB], AVST[:]),
                                    sem_sh, [E(mu)])
                                at_state["shift_prev"] = ("d", shift_ev.dma_sem, shift_ev.post)
                            pair_evacs.append(E(mu))
                        at_state["bc_prev"] = prev_bc
                        if shift_ev is not None:
                            pair_evacs.append(("d", shift_ev.dma_sem, shift_ev.post))
                        at_state["pair_evacs"] = pair_evacs

                    if qb == 0:
                        wo_dma = tr.dma("gpsimd", lambda e, hg=hg: e.dma_start(
                            WQ2B[:, 4 * D:8 * D].rearrange("p (dt c) -> p dt c", c=D),
                            w_out.rearrange("(dt p) c -> p dt c", p=128)
                            [:, hg * 4:hg * 4 + 4, :]),
                            sem_qk[1], [E(v_evacs[-1])])
                    for tt in range(4):
                        for fb in range(2):
                            for fp in range(FPG):
                                waits = []
                                if fp == 0:
                                    waits = [("d", wo_dma.dma_sem, wo_dma.post),
                                             E(at_state["pe_prev"])]
                                    if tt == 0 and fb == 0:
                                        waits += at_state["pair_evacs"]
                                pm = tr.add("tensor", lambda e, tt=tt, fb=fb, fp=fp: e.matmul(
                                    PD[:, 0:512],
                                    AVT[:, fp * QB + tt * 128:fp * QB + (tt + 1) * 128],
                                    WQ2B[:, 4 * D + fp * D + fb * 512:
                                         4 * D + fp * D + (fb + 1) * 512],
                                    start=(fp == 0), stop=(fp == FPG - 1)), waits)
                            gtt = qb * 4 + tt
                            pe = tr.add("vector", lambda e, gtt=gtt, fb=fb:
                                        e.scalar_tensor_tensor(
                                X[:, gtt * D + fb * 512:gtt * D + (fb + 1) * 512],
                                PD[:, 0:512], ONEF[:, 0:1],
                                X[:, gtt * D + fb * 512:gtt * D + (fb + 1) * 512],
                                ALU.mult, ALU.add), [E(pm), E(preadd[gtt])])
                            at_state["pe_prev"] = pe
                            last_proj.append(pe)
            return last_proj

        for b in range(nb):
            xw = [] if b == 0 else [("d", sem_y, sem_y.count * 16)]
            xls = []
            for tt in range(NT):
                xls.append(tr.dma("sync", lambda e, b=b, tt=tt: e.dma_start(
                    X[:, tt * D:(tt + 1) * D],
                    x_in.rearrange("(n p) c -> n p c", p=128)[b * NT + tt]),
                    sem_x, xw if tt == 0 else []))
            x_ready = [("d", sem_x, sem_x.count * 16)]
            arena_free = list(prep_done) if b == 0 else []
            if "f1" in phases:
                evs = ffn(x_ready, arena_free)
                x_ready = [E(o) for o in evs]
                arena_free = []
            if "at" in phases:
                evs = attn(x_ready, arena_free)
                x_ready = [E(o) for o in evs[-8:]]
                arena_free = []
            if "f2" in phases:
                evs = ffn(x_ready, arena_free)
                x_ready = [E(o) for o in evs]
            for tt in range(NT):
                tr.dma("sync", lambda e, b=b, tt=tt: e.dma_start(
                    y_out.rearrange("(n p) c -> n p c", p=128)[b * NT + tt],
                    X[:, tt * D:(tt + 1) * D]), sem_y, x_ready if tt == 0 else [])

        tr.add("sync", lambda e: e.nop(), [("d", sem_y, sem_y.count * 16)])
        tr.emit(nc, ctx)
    nc._dbg = dict(X=X, XQ=XQ, XQT=XQT, S1C=S1C, AMAX1=AMAX1, F1C=F1C, F1ROW=F1ROW,
                   H2=H2, XQ2=XQ2, AMAX2=AMAX2, S2CC=S2CC, FAC2=FAC2, WSC=WSC,
                   AVT=AVT, ARBB=ARBB, KT=KT, STATS=STATS, S2B=S2B, F1B=F1B,
                   LN1=LN1, QR=QR, ROWS=ROWS, NEGB=NEGB)
    return nc


_cache = {}


def _get_program(phases=("f1", "at", "f2")):
    key = tuple(phases)
    if key not in _cache:
        _cache[key] = build_program(phases)
    return _cache[key]


def _host_layouts(L0_attn_w_qkv, L0_attn_b_qkv, L0_attn_w_out, L0_attn_b_out,
                  L0_ffn_w1, L0_ffn_conv_w, L0_ffn_conv_b, L0_ffn_w2):
    wq = np.ascontiguousarray(np.asarray(L0_attn_w_qkv, np.float32))
    bq = np.asarray(L0_attn_b_qkv, np.float32).reshape(3 * D)
    bqcol = np.ascontiguousarray(bq[:3072].reshape(24, 128).T)
    bqv_row = np.ascontiguousarray(bq[2 * D:3 * D].reshape(1, D))
    bout_row = np.ascontiguousarray(np.asarray(L0_attn_b_out, np.float32).reshape(1, D))
    wo = np.ascontiguousarray(np.asarray(L0_attn_w_out, np.float32))
    w1 = np.ascontiguousarray(np.asarray(L0_ffn_w1, np.float32))
    cw = np.asarray(L0_ffn_conv_w, np.float32).reshape(HID, 3)
    cw_pt = np.ascontiguousarray(cw.reshape(NCT, 128, 3).transpose(1, 0, 2).reshape(128, NCT * 3))
    cb = np.asarray(L0_ffn_conv_b, np.float32).reshape(HID)
    cb_pt = np.ascontiguousarray(cb.reshape(NCT, 128).T)
    w2 = np.ascontiguousarray(np.asarray(L0_ffn_w2, np.float32))
    return dict(w_qkv=wq, bqcol=bqcol, bqv_row=bqv_row, bout_row=bout_row,
                w_out=wo, w1=w1, cw_pt=cw_pt, cb_pt=cb_pt, w2=w2)


def kernel(L0_attn_x, L0_attn_w_qkv, L0_attn_b_qkv, L0_attn_w_out, L0_attn_b_out,
           L0_ffn_w1, L0_ffn_conv_w, L0_ffn_conv_b, L0_ffn_w2):
    x = np.ascontiguousarray(np.asarray(L0_attn_x, dtype=np.float32)).reshape(16 * S, D)
    shards = np.split(x, NCORES, axis=0)
    wmap = _host_layouts(L0_attn_w_qkv, L0_attn_b_qkv, L0_attn_w_out, L0_attn_b_out,
                         L0_ffn_w1, L0_ffn_conv_w, L0_ffn_conv_b, L0_ffn_w2)
    nc = _get_program()
    in_maps = [dict(x=s, **wmap) for s in shards]
    res = run_bass_kernel_spmd(nc, in_maps, core_ids=list(range(NCORES)))
    y = np.concatenate([r["y"] for r in res.results], axis=0)
    return y.reshape(16, S, D)
